# version 01
# baseline (speedup 1.0000x reference)
import numpy as np
import ml_dtypes
import concourse.bass as bass
import concourse.mybir as mybir
from concourse.bass_utils import run_bass_kernel_spmd

F32 = mybir.dt.float32
BF16 = mybir.dt.bfloat16
I32 = mybir.dt.int32
U8 = mybir.dt.uint8
AF = mybir.ActivationFunctionType
ALU = mybir.AluOpType
AX = mybir.AxisListType
bfdt = ml_dtypes.bfloat16

HID = 4096
DIN = 8192
NST = 16
RDT = 256
INTER = 11008
LPRE = 2048
LOWN = 1024
TB = 512
HC = HID // 128
DC = DIN // 128
ICN = INTER // 128
EPS_LN = 1e-5
EPS_RMS = 1e-6
LSEG = 512

EV_STEP, EV_DMA = 0, 1


class Sched:
    ENG = ("sync", "tensor", "vector", "scalar", "gpsimd")

    def __init__(self, nc):
        self.nc = nc
        self.sem = {e: nc.alloc_semaphore(f"sem_{e}") for e in self.ENG}
        self.cnt = {e: 0 for e in self.ENG}
        self.dsem = {}
        self.dcnt = {}
        self.body = {e: [] for e in self.ENG}
        self.drained = {e: 0 for e in self.ENG}
        self.waited = {}
        self.w = {}
        self.rd = {}
        self.nsteps = 0

    def _dstream(self, name):
        if name not in self.dsem:
            self.dsem[name] = self.nc.alloc_semaphore(f"ds_{name}")
            self.dcnt[name] = 0
        return self.dsem[name]

    def _deps(self, reads, writes):
        deps = []
        for r in reads:
            ev = self.w.get(r)
            if ev is not None:
                deps.append(ev)
        for r in writes:
            ev = self.w.get(r)
            if ev is not None:
                deps.append(ev)
            deps.extend(self.rd.get(r, ()))
        return deps

    def _emit_sync(self, e, deps):
        need_drain = False
        waits = {}
        for kind, src_engine, val, semkey in deps:
            if kind == EV_STEP and src_engine == e:
                if val > self.drained[e]:
                    need_drain = True
                continue
            if kind == EV_DMA:
                val = self.dcnt[semkey[1]]
            if self.waited.get((e, semkey), 0) >= val:
                continue
            waits[semkey] = max(waits.get(semkey, 0), val)
        if need_drain:
            self.body[e].append(lambda eng: eng.drain())
            self.drained[e] = self.cnt[e]
        for semkey, val in waits.items():
            sem = self.sem[semkey[1]] if semkey[0] == "S" else self.dsem[semkey[1]]
            self.body[e].append(lambda eng, s=sem, v=val: eng.wait_ge(s, v))
            self.waited[(e, semkey)] = val

    def step(self, e, fn, reads=(), writes=(), tag=""):
        self._emit_sync(e, self._deps(reads, writes))
        self.cnt[e] += 1
        myval = self.cnt[e]
        sem = self.sem[e]

        def run(eng, fn=fn, sem=sem, tag=tag):
            last = fn(eng)
            assert last is not None, f"step fn returned None ({tag})"
            last.then_inc(sem, 1)
        self.body[e].append(run)
        me = (EV_STEP, e, myval, ("S", e))
        for r in writes:
            self.w[r] = me
            self.rd[r] = []
        for r in reads:
            self.rd.setdefault(r, []).append(me)
        self.nsteps += 1
        return me

    def dma(self, e, out, in_, reads=(), writes=(), stream=None, transpose=False, tag=""):
        assert not (transpose and e != "sync"), "xbar transposes: sync queue only"
        stream = stream or ("q_" + e)
        sem = self._dstream(stream)
        self._emit_sync(e, self._deps(reads, writes))
        self.dcnt[stream] += 16
        myval = self.dcnt[stream]

        def run(eng, out=out, in_=in_, sem=sem, transpose=transpose):
            if transpose:
                eng.dma_start_transpose(out=out, in_=in_).then_inc(sem, 16)
            else:
                eng.dma_start(out=out, in_=in_).then_inc(sem, 16)
        self.body[e].append(run)
        me = (EV_DMA, e, myval, ("D", stream))
        for r in writes:
            self.w[r] = me
            self.rd[r] = []
        for r in reads:
            self.rd.setdefault(r, []).append(me)
        self.nsteps += 1
        return me

    def raw(self, e, fn):
        self.body[e].append(fn)

    def fence(self, e, streams, names):
        for s in streams:
            val = self.dcnt.get(s, 0)
            if val and self.waited.get((e, ("D", s)), 0) < val:
                sem = self.dsem[s]
                self.body[e].append(lambda eng, s=sem, v=val: eng.wait_ge(s, v))
                self.waited[(e, ("D", s))] = val
        self.cnt[e] += 1
        myval = self.cnt[e]
        sem = self.sem[e]
        self.body[e].append(lambda eng, s=sem: eng.nop().then_inc(s, 1))
        me = (EV_STEP, e, myval, ("S", e))
        for n in names:
            self.w[n] = me
            self.rd[n] = []

    def hard_barrier(self):
        for e in self.ENG:
            self.cnt[e] += 1
            sem = self.sem[e]
            self.body[e].append(lambda eng, s=sem: eng.drain().then_inc(s, 1))
            self.drained[e] = self.cnt[e]
        e = "sync"
        for o in self.ENG:
            if o != e:
                val = self.cnt[o]
                self.body[e].append(
                    lambda eng, s=self.sem[o], v=val: eng.wait_ge(s, v))
                self.waited[(e, ("S", o))] = val
        for stream, val in list(self.dcnt.items()):
            if val > self.waited.get((e, ("D", stream)), 0):
                self.body[e].append(
                    lambda eng, s=self.dsem[stream], v=val: eng.wait_ge(s, v))
                self.waited[(e, ("D", stream))] = val
        self.cnt[e] += 1
        pub = self.cnt[e]
        sem_e = self.sem[e]
        self.body[e].append(lambda eng, s=sem_e: eng.nop().then_inc(s, 1))
        for o in self.ENG:
            if o == e:
                continue
            self.body[o].append(lambda eng, s=sem_e, v=pub: eng.wait_ge(s, v))
            self.waited[(o, ("S", e))] = pub
        me = (EV_STEP, e, pub, ("S", e))
        for r in list(self.w):
            self.w[r] = me
        for r in list(self.rd):
            self.rd[r] = []

    def finalize(self):
        with self.nc.Block() as block:
            for e in self.ENG:
                lst = self.body[e]
                if not lst:
                    continue
                starter = {"sync": block.sync, "tensor": block.tensor,
                           "vector": block.vector, "scalar": block.scalar,
                           "gpsimd": block.gpsimd}[e]

                @starter
                def _(eng, lst=lst):
                    for fn in lst:
                        fn(eng)


ARENA_BYTES = 180224
OFF_SIL = 163840
OFF_HOUT = 168192
OFF_LNT = 172032
OFF_DTNFM = 172032


class Arena:
    def __init__(self, nc):
        self.buf = nc.alloc_sbuf_tensor("arena", [128, ARENA_BYTES], U8)

    def view(self, off, dt, cols):
        nb = {F32: 4, BF16: 2, I32: 4}[dt]
        assert off % 64 == 0 and off + cols * nb <= ARENA_BYTES, (off, cols, dt)
        return self.buf[:, off:off + cols * nb].bitcast(dt)


def emit_dequant(S, t, A, ap, ps, mat, N, K, G):
    K8 = K // 8
    N8 = N // 8
    qw, sc = ap[f"qw_{mat}"], ap[f"sc_{mat}"]
    dst = ap[f"w{mat}_s"]
    szt_s = ap[f"szt_{mat}"]
    KSEG = 2048
    segs = [(s, min(KSEG, K - s)) for s in range(0, K, KSEG)]
    NSEG = 2048
    qzb = A.view(38144, I32, 256)
    z4b = A.view(39232, I32, NSEG)
    zfb = A.view(47424, F32, NSEG)
    sgb = A.view(55616, F32, NSEG)
    szo = [A.view(63808 + i * 512, F32, 128) for i in range(2)]
    for s0 in range(0, N, NSEG):
        ns = min(NSEG, N - s0)
        S.dma("sync", qzb[0:G, 0:ns // 8], ap[f"qz_{mat}"][:, s0 // 8:(s0 + ns) // 8],
              writes=("qzb",), stream="qd_z0")
        S.dma("sync", sgb[0:G, 0:ns], ap[f"sg_{mat}"][:, s0:s0 + ns],
              writes=("sgb",), stream="qd_z1")
        for j in range(8):
            S.step("vector", lambda eng, j=j, ns=ns: eng.tensor_scalar(
                out=z4b[0:G, 0:ns].rearrange("p (n8 j) -> p n8 j", j=8)[:, :, j],
                in0=qzb[0:G, 0:ns // 8], scalar1=4 * j, scalar2=15,
                op0=ALU.logical_shift_right, op1=ALU.bitwise_and),
                reads=("qzb",), writes=("z4b",))
        S.step("vector", lambda eng, ns=ns: eng.tensor_scalar(
            out=zfb[0:G, 0:ns], in0=z4b[0:G, 0:ns], scalar1=1.0, scalar2=-1.0,
            op0=ALU.add, op1=ALU.mult), reads=("z4b",), writes=("zfb",))
        S.step("vector", lambda eng, ns=ns: eng.tensor_tensor(
            out=zfb[0:G, 0:ns], in0=zfb[0:G, 0:ns], in1=sgb[0:G, 0:ns],
            op=ALU.mult), reads=("zfb", "sgb"), writes=("zfb",))
        for ci in range(ns // 128):
            par = ci % 2
            S.step("tensor", lambda eng, ci=ci: eng.transpose(
                ps["p5"][:, 0:G],
                zfb[0:G, ci * 128:(ci + 1) * 128], t["eye_f32"][0:G, 0:G]),
                reads=("zfb", "eye_f32"), writes=("p5",), tag="szT")
            S.step("vector", lambda eng, par=par: eng.tensor_copy(
                out=szo[par][:, 0:G], in_=ps["p5"][:, 0:G]),
                reads=("p5",), writes=(f"szo{par}",))
            S.dma("sync", szt_s[s0 + ci * 128:s0 + (ci + 1) * 128, :],
                  szo[par][:, 0:G], reads=(f"szo{par}",),
                  writes=(f"szt/{(s0 + ci * 128) // 128}",), stream=f"qd_szo{par}")
    qwb_v = [A.view(0, I32, 1376), A.view(5632, I32, 1376)]
    sb_v = [A.view(11264, F32, 86), A.view(11648, F32, 86)]
    szb_v = [A.view(12800, F32, 86), A.view(13184, F32, 86)]
    w4_v = A.view(13568, I32, KSEG)
    wf_v = A.view(21760, F32, KSEG)
    wo_v = [A.view(29952, BF16, KSEG), A.view(34048, BF16, KSEG)]
    for ni in range(N // 128):
        b = ni % 2
        qwb, sb_, szb = qwb_v[b], sb_v[b], szb_v[b]
        rows = slice(ni * 128, (ni + 1) * 128)
        pre = f"dq{b}"
        S.dma("sync", qwb[:, 0:K8], qw[rows, :], writes=(pre + "qw",), stream=f"qd_qw{b}")
        S.dma("sync", sb_[:, 0:G], sc[rows, :], writes=(pre + "s",), stream=f"qd_s{b}")
        S.dma("sync", szb[:, 0:G], szt_s[rows, :], reads=(f"szt/{ni}",),
              writes=(pre + "sz",), stream=f"qd_zz{b}")
        for si, (k0, klen) in enumerate(segs):
            g0, gn = k0 // 128, klen // 128
            wo = wo_v[si % 2]
            wos = f"dqwo{si % 2}"
            for j in range(8):
                S.step("vector", lambda eng, qwb=qwb, j=j, k0=k0, klen=klen: eng.tensor_scalar(
                    out=w4_v[:, 0:klen].rearrange("p (k8 j) -> p k8 j", j=8)[:, :, j],
                    in0=qwb[:, k0 // 8:(k0 + klen) // 8], scalar1=4 * j, scalar2=15,
                    op0=ALU.logical_shift_right, op1=ALU.bitwise_and),
                    reads=(pre + "qw",), writes=("dqw4",))
            S.step("vector", lambda eng, sb_=sb_, klen=klen, g0=g0, gn=gn: eng.tensor_tensor(
                out=wf_v[:, 0:klen].rearrange("p (g i) -> p g i", g=gn),
                in0=w4_v[:, 0:klen].rearrange("p (g i) -> p g i", g=gn),
                in1=sb_[:, g0:g0 + gn].unsqueeze(2).broadcast_to([128, gn, 128]),
                op=ALU.mult), reads=("dqw4", pre + "s"), writes=("dqwf",))
            S.step("vector", lambda eng, szb=szb, wo=wo, klen=klen, g0=g0, gn=gn: eng.tensor_tensor(
                out=wo[:, 0:klen].rearrange("p (g i) -> p g i", g=gn),
                in0=wf_v[:, 0:klen].rearrange("p (g i) -> p g i", g=gn),
                in1=szb[:, g0:g0 + gn].unsqueeze(2).broadcast_to([128, gn, 128]),
                op=ALU.add), reads=("dqwf", pre + "sz"), writes=(wos,))
            S.dma("sync", dst[rows, k0:k0 + klen], wo[:, 0:klen], reads=(wos,),
                  writes=(f"w{mat}_s/{ni}",), stream=f"qd_wo{si % 2}")


def emit_ln(S, t, ps, x_dram, tb, xn_dst, xres, uid):
    cols = slice(tb * TB, (tb + 1) * TB)
    pmu, pms, pb1, pb2 = ps["p0"], ps["p1"], ps["p2"], ps["p3"]
    tmp = t["lntmp"]
    rows_v = t["lnrows"]
    for p2 in range(2):
        for kc in range(HC):
            b = kc % 2
            tile = tmp[b]
            tn = f"lntmp{b}"
            S.dma("gpsimd", tile, x_dram[kc * 128:(kc + 1) * 128, cols],
                  reads=(xres(kc),), writes=(tn,), stream=tn)
            if p2 == 0:
                S.step("tensor", lambda eng, tile=tile, kc=kc: eng.matmul(
                    out=pmu[0:1, 0:TB], lhsT=t["ones_col"][:], rhs=tile,
                    start=(kc == 0), stop=(kc == HC - 1)),
                    reads=(tn, "ones_col"), writes=("p0",), tag="lnmu")
                S.step("scalar", lambda eng, tile=tile, b=b: eng.activation(
                    out=tmp[2 + b], in_=tile, func=AF.Square),
                    reads=(tn,), writes=(f"lnsq{b}",))
                S.step("tensor", lambda eng, b=b, kc=kc: eng.matmul(
                    out=pms[0:1, 0:TB], lhsT=t["ones_col"][:], rhs=tmp[2 + b],
                    start=(kc == 0), stop=(kc == HC - 1)),
                    reads=(f"lnsq{b}", "ones_col"), writes=("p1",), tag="lnms")
            else:
                S.step("vector", lambda eng, tile=tile: eng.tensor_tensor(
                    out=tile, in0=tile, in1=pb1[:, 0:TB], op=ALU.mult),
                    reads=(tn, "p2"), writes=(tn,))
                S.step("vector", lambda eng, tile=tile, kc=kc: eng.tensor_tensor(
                    out=xn_dst(kc), in0=tile, in1=pb2[:, 0:TB], op=ALU.add),
                    reads=(tn, "p3"), writes=(f"xn{kc}t{uid}",))
        if p2 == 0:
            r_mu = rows_v[:, 0:TB]
            r_b = rows_v[:, TB:2 * TB]
            S.step("vector", lambda eng: eng.tensor_scalar(
                out=r_mu, in0=pmu[0:1, 0:TB], scalar1=1.0 / HID, scalar2=None,
                op0=ALU.mult), reads=("p0",), writes=("rmu",))
            S.step("vector", lambda eng: eng.scalar_tensor_tensor(
                out=r_b, in0=pmu[0:1, 0:TB], scalar=1.0 / HID, in1=r_mu,
                op0=ALU.mult, op1=ALU.mult),
                reads=("p0", "rmu"), writes=("rb",))
            S.step("vector", lambda eng: eng.scalar_tensor_tensor(
                out=r_b, in0=pms[0:1, 0:TB], scalar=1.0 / HID, in1=r_b,
                op0=ALU.mult, op1=ALU.subtract),
                reads=("p1", "rb"), writes=("rb",))
            S.step("scalar", lambda eng: eng.activation(
                out=r_b, in_=r_b, func=AF.Ln, bias=t["cconst"][0:1, 0:1]),
                reads=("rb", "cconst"), writes=("rb",))
            S.step("scalar", lambda eng: eng.activation(
                out=r_b, in_=r_b, func=AF.Exp, scale=t["cconst"][0:1, 2:3]),
                reads=("rb", "cconst"), writes=("rb",))
            S.step("tensor", lambda eng: eng.matmul(
                out=pb1[:, 0:TB], lhsT=t["ones_row"][:], rhs=r_b,
                start=True, stop=True), reads=("rb", "ones_row"),
                writes=("p2",), tag="lnbc1")
            S.step("vector", lambda eng: eng.scalar_tensor_tensor(
                out=r_mu, in0=r_mu, scalar=-1.0, in1=r_b, op0=ALU.mult,
                op1=ALU.mult), reads=("rmu", "rb"), writes=("rmu",))
            S.step("tensor", lambda eng: eng.matmul(
                out=pb2[:, 0:TB], lhsT=t["ones_row"][:], rhs=r_mu,
                start=True, stop=True), reads=("rmu", "ones_row"),
                writes=("p3",), tag="lnbc2")


def emit_ffn(S, t, A, ap, ps, x_dram, out_dram, xres, outres, tbs, uid):
    ntb = len(tbs)
    xn = A.view(0, BF16, HC * ntb * TB)
    sgo = HC * ntb * TB * 2
    slabG = A.view(sgo, BF16, 8192)
    slabU = A.view(sgo + 16384, BF16, 8192)
    h1s = ap["h1_s"]

    def xn_ap(kc, tbi):
        return xn[:, (kc * ntb + tbi) * TB: (kc * ntb + tbi + 1) * TB]

    for tbi, tb in enumerate(tbs):
        emit_ln(S, t, ps, x_dram, tb, lambda kc, tbi=tbi: xn_ap(kc, tbi),
                xres, f"{uid}{tbi}")
    for icg in range(ICN // 2):
        nrows = slice(icg * 256, (icg + 1) * 256)
        for kc in range(HC):
            S.dma("sync", slabG[:, kc * 256:(kc + 1) * 256],
                  ap["wg_s"][nrows, kc * 128:(kc + 1) * 128],
                  reads=(f"wg_s/{icg * 2}", f"wg_s/{icg * 2 + 1}"),
                  writes=("slabG",) if kc == HC - 1 else ("slabGp",),
                  transpose=True, stream="qs_sG")
            S.dma("sync", slabU[:, kc * 256:(kc + 1) * 256],
                  ap["wu_s"][nrows, kc * 128:(kc + 1) * 128],
                  reads=(f"wu_s/{icg * 2}", f"wu_s/{icg * 2 + 1}"),
                  writes=("slabU",) if kc == HC - 1 else ("slabUp",),
                  transpose=True, stream="qs_sU")
        for sub in range(2):
            ic = icg * 2 + sub
            for tbi, tb in enumerate(tbs):
                par = (ic * ntb + tbi) % 2
                pga, rga = (ps["p0"], "p0") if par == 0 else (ps["p2"], "p2")
                pua, rua = (ps["p1"], "p1") if par == 0 else (ps["p3"], "p3")
                xnr = tuple(f"xn{kc}t{uid}{tbi}" for kc in range(HC))
                S.step("tensor", lambda eng, sub=sub, tbi=tbi, pga=pga: [
                    eng.matmul(out=pga[:, 0:TB],
                               lhsT=slabG[:, kc * 256 + sub * 128: kc * 256 + sub * 128 + 128],
                               rhs=xn_ap(kc, tbi), start=(kc == 0), stop=(kc == HC - 1))
                    for kc in range(HC)][-1],
                    reads=("slabG", "slabGp") + xnr, writes=(rga,), tag="mmg")
                S.step("tensor", lambda eng, sub=sub, tbi=tbi, pua=pua: [
                    eng.matmul(out=pua[:, 0:TB],
                               lhsT=slabU[:, kc * 256 + sub * 128: kc * 256 + sub * 128 + 128],
                               rhs=xn_ap(kc, tbi), start=(kc == 0), stop=(kc == HC - 1))
                    for kc in range(HC)][-1],
                    reads=("slabU", "slabUp") + xnr, writes=(rua,), tag="mmu")
                sb = t["sil"][par]
                ho = t["hout"][(ic * ntb + tbi) % 4]
                hor = f"hout{(ic * ntb + tbi) % 4}"
                S.step("scalar", lambda eng, pga=pga, sb=sb: eng.activation(
                    out=sb, in_=pga[:, 0:TB], func=AF.Silu),
                    reads=(rga,), writes=(f"sil{par}",))
                S.step("vector", lambda eng, sb=sb, pua=pua, ho=ho: eng.tensor_tensor(
                    out=ho, in0=sb, in1=pua[:, 0:TB], op=ALU.mult),
                    reads=(f"sil{par}", rua), writes=(hor,))
                S.dma("gpsimd", h1s[ic * 128:(ic + 1) * 128, tb * TB:(tb + 1) * TB],
                      ho, reads=(hor,), writes=(f"h1_s/{ic}t{tb}",), stream=hor)
    S.hard_barrier()
    h1tb = A.view(0, BF16, ICN * TB)
    slabD = [A.view(90112 + i * 1024, BF16, TB) for i in range(2)]
    xt = [A.view(92160 + i * 2048, F32, TB) for i in range(2)]
    ot = [A.view(96256 + i * 2048, F32, TB) for i in range(2)]
    pd = [ps["p0"], ps["p1"], ps["p2"], ps["p3"]]
    for tb in tbs:
        for ic in range(ICN):
            S.dma("sync", h1tb[:, ic * TB:(ic + 1) * TB],
                  h1s[ic * 128:(ic + 1) * 128, tb * TB:(tb + 1) * TB],
                  reads=(f"h1_s/{ic}t{tb}",), writes=(f"h1tb{ic}",), stream="qs_h1")
        S.fence("tensor", ["qs_h1"], [f"h1tb{ic}" for ic in range(ICN)])
        for ocg in range(8):
            for ic in range(ICN):
                sd = slabD[ic % 2]
                rd_ = f"slabD{ic % 2}"
                S.dma("sync", sd,
                      ap["wd_s"][ocg * 512:(ocg + 1) * 512, ic * 128:(ic + 1) * 128],
                      reads=(f"wd_s/{ocg * 4}", f"wd_s/{ocg * 4 + 1}",
                             f"wd_s/{ocg * 4 + 2}", f"wd_s/{ocg * 4 + 3}"),
                      writes=(rd_,), transpose=True, stream=f"qs_sD{ic % 2}")
                S.step("tensor", lambda eng, sd=sd, ic=ic: [
                    eng.matmul(out=pd[oc][:, 0:TB],
                               lhsT=sd[:, oc * 128:(oc + 1) * 128],
                               rhs=h1tb[:, ic * TB:(ic + 1) * TB],
                               start=(ic == 0), stop=(ic == ICN - 1))
                    for oc in range(4)][-1],
                    reads=(rd_, f"h1tb{ic}"),
                    writes=("p0", "p1", "p2", "p3"), tag="mmd")
            for oc in range(4):
                hc = ocg * 4 + oc
                xb = xt[oc % 2]
                ob = ot[oc % 2]
                S.dma("gpsimd", xb, x_dram[hc * 128:(hc + 1) * 128, tb * TB:(tb + 1) * TB],
                      reads=(xres(hc),), writes=(f"xt{oc % 2}",), stream=f"xt{oc % 2}")
                S.step("vector", lambda eng, xb=xb, ob=ob, oc=oc: eng.scalar_tensor_tensor(
                    out=ob, in0=pd[oc][:, 0:TB], scalar=0.5, in1=xb,
                    op0=ALU.mult, op1=ALU.add),
                    reads=(f"p{oc}", f"xt{oc % 2}"), writes=(f"ot{oc % 2}",))
                S.dma("gpsimd", out_dram[hc * 128:(hc + 1) * 128, tb * TB:(tb + 1) * TB],
                      ob, reads=(f"ot{oc % 2}",), writes=(outres(hc, tb),),
                      stream=f"ot{oc % 2}")
    S.hard_barrier()


def emit_win(S, t, A, ap, ps):
    ntb = 4
    xn = A.view(0, BF16, HC * ntb * TB)
    slabs = [A.view(131072 + i * 16384, BF16, 8192) for i in range(2)]
    ut = [A.view(OFF_SIL + i * 2048, F32, TB) for i in range(2)]

    def xn_ap(kc, tbi):
        return xn[:, (kc * ntb + tbi) * TB: (kc * ntb + tbi + 1) * TB]

    for tb in range(4):
        emit_ln(S, t, ps, ap["x1_s"], tb, lambda kc, tbi=tb: xn_ap(kc, tbi),
                lambda kc: f"x1_s/{kc}", f"W{tb}")
    pcnt = 0
    for rg in range(64):
        sl = slabs[rg % 2]
        rsl = f"slabW{rg % 2}"
        for cc in range(HC):
            S.dma("gpsimd", sl[:, cc * 256:(cc + 1) * 256],
                  ap["win_t"][cc * 128:(cc + 1) * 128, rg * 256:(rg + 1) * 256],
                  writes=(rsl,) if cc == HC - 1 else (rsl + "p",),
                  stream=f"qg_win{rg % 2}")
        for rsub in range(2):
            rc = rg * 2 + rsub
            tb_list = range(4) if rc < DC else range(2, 4)
            for tb in tb_list:
                par = pcnt % 2
                pcnt += 1
                pp, rp = (ps["p0"], "p0") if par == 0 else (ps["p1"], "p1")
                xnr = tuple(f"xn{kc}tW{tb}" for kc in range(HC))
                S.step("tensor", lambda eng, sl=sl, rsub=rsub, tb=tb, pp=pp: [
                    eng.matmul(out=pp[:, 0:TB],
                               lhsT=sl[:, kc * 256 + rsub * 128: kc * 256 + rsub * 128 + 128],
                               rhs=xn_ap(kc, tb), start=(kc == 0), stop=(kc == HC - 1))
                    for kc in range(HC)][-1],
                    reads=(rsl, rsl + "p") + xnr, writes=(rp,), tag="mmwin")
                ub = ut[par]
                if rc < DC:
                    S.step("vector", lambda eng, pp=pp, ub=ub: eng.tensor_copy(
                        out=ub, in_=pp[:, 0:TB]),
                        reads=(rp,), writes=(f"ut{par}",))
                    S.dma("gpsimd", ap["ur_s"][rc * 128:(rc + 1) * 128, tb * TB:(tb + 1) * TB],
                          ub, reads=(f"ut{par}",), writes=(f"ur_s/{rc}t{tb}",),
                          stream=f"qg_ur{par}")
                else:
                    S.step("scalar", lambda eng, pp=pp, ub=ub: eng.activation(
                        out=ub, in_=pp[:, 0:TB], func=AF.Silu),
                        reads=(rp,), writes=(f"ut{par}",))
                    S.dma("gpsimd",
                          ap["sg_s"][(rc - DC) * 128:(rc - DC + 1) * 128,
                                     (tb - 2) * TB:(tb - 1) * TB],
                          ub, reads=(f"ut{par}",), writes=(f"sg_s/{rc - DC}t{tb - 2}",),
                          stream=f"qg_ur{par}")
    S.hard_barrier()


def emit_conv_ssm(S, t, A, ap, ps):
    urp = [A.view(0 + i * 4160, F32, 1028) for i in range(2)]
    ct = [A.view(8320 + i * 4096, F32, 1024) for i in range(2)]
    u2t = [A.view(16512 + i * 4096, F32, 1024) for i in range(2)]
    u2b = [A.view(24704 + i * 2048, BF16, 1024) for i in range(2)]
    wxa = A.view(28800, BF16, DC * 288)
    sqt = [A.view(65664 + i * 1024, F32, 256) for i in range(2)]
    sqb = [A.view(67712 + i * 128, F32, 32) for i in range(2)]
    rms3 = A.view(67968, F32, 16)
    dtnt = [A.view(68032, BF16, 256), A.view(68544, BF16, 256)]
    bnt = [A.view(69056 + i * 64, F32, 16) for i in range(4)]
    dtn_fm = A.view(OFF_DTNFM, BF16, 2 * LPRE)
    S.dma("gpsimd", wxa[:, 0:DC * 288].rearrange("p (d s) -> p d s", s=288),
          ap["wx_t"].rearrange("(d p) s -> p d s", p=128), writes=("wxa",),
          stream="qg_wx")
    for th in range(2):
        for dc in range(DC):
            b = dc % 2
            up, cb, ub = urp[b], ct[b], u2t[b]
            run, rc_, ru = f"urp{b}", f"ct{b}", f"u2t{b}"
            rows = slice(dc * 128, (dc + 1) * 128)
            if th == 0:
                S.step("vector", lambda eng, up=up: eng.memset(up[:, 0:3], 0.0),
                       reads=(), writes=(run,))
                S.dma("gpsimd", up[:, 3:1027], ap["ur_s"][rows, 0:1024],
                      reads=(f"ur_s/{dc}t0", f"ur_s/{dc}t1"), writes=(run + "d",),
                      stream=f"qg_u{b}")
            else:
                S.dma("gpsimd", up[:, 0:1027], ap["ur_s"][rows, 1021:2048],
                      reads=(f"ur_s/{dc}t1", f"ur_s/{dc}t2", f"ur_s/{dc}t3"),
                      writes=(run, run + "d"), stream=f"qg_u{b}")
            S.step("vector", lambda eng, up=up, cb=cb, dc=dc: eng.tensor_scalar(
                out=cb[:, 0:1024], in0=up[:, 0:1024],
                scalar1=t["conv_sb"][:, dc * 4:dc * 4 + 1], scalar2=None, op0=ALU.mult),
                reads=(run, run + "d", "conv_sb"), writes=(rc_,))
            for k in range(1, 4):
                S.step("vector", lambda eng, up=up, cb=cb, dc=dc, k=k: eng.scalar_tensor_tensor(
                    out=cb[:, 0:1024], in0=up[:, k:1024 + k],
                    scalar=t["conv_sb"][:, dc * 4 + k:dc * 4 + k + 1],
                    in1=cb[:, 0:1024], op0=ALU.mult, op1=ALU.add),
                    reads=(run, run + "d", rc_, "conv_sb"), writes=(rc_,))
            S.step("scalar", lambda eng, cb=cb, ub=ub, dc=dc: eng.activation(
                out=ub[:, 0:1024], in_=cb[:, 0:1024], func=AF.Silu,
                bias=t["convb_sb"][:, dc:dc + 1]),
                reads=(rc_, "convb_sb"), writes=(ru,))
            S.dma("gpsimd", ap["u2_s"][rows, th * 1024:(th + 1) * 1024], ub[:, 0:1024],
                  reads=(ru,), writes=(f"u2_s/{dc}h{th}",), stream=f"qg_u2{b}")
    for tq in range(4):
        for dc in range(DC):
            b = dc % 2
            uB = u2b[b]
            rub = f"u2b{b}"
            rows = slice(dc * 128, (dc + 1) * 128)
            S.dma("gpsimd", uB[:, 0:512], ap["u2_s"][rows, tq * 512:(tq + 1) * 512],
                  reads=(f"u2_s/{dc}h{tq // 2}",), writes=(rub,), stream=f"qg_ub{b}")
            for tcl in range(4):
                S.step("tensor", lambda eng, uB=uB, tcl=tcl, dc=dc: eng.matmul(
                    out=ps[f"p{tcl}"][:, 0:288],
                    lhsT=uB[:, tcl * 128:(tcl + 1) * 128],
                    rhs=wxa[:, dc * 288:(dc + 1) * 288],
                    start=(dc == 0), stop=(dc == DC - 1)),
                    reads=(rub, "wxa"), writes=(f"p{tcl}",), tag="mmssm")
        for tcl in range(4):
            b = tcl % 2
            tc_g = tq * 4 + tcl
            pp = ps[f"p{tcl}"]
            rp = f"p{tcl}"
            sq, sb_ = sqt[b], sqb[b]
            rsq, rsb = f"sqt{b}", f"sqb{b}"
            S.step("scalar", lambda eng, sq=sq, pp=pp: eng.activation(
                out=sq[:, 0:256], in_=pp[:, 0:256], func=AF.Square),
                reads=(rp,), writes=(rsq,))
            S.step("scalar", lambda eng, sb_=sb_, pp=pp: eng.activation(
                out=sb_[:, 0:32], in_=pp[:, 256:288], func=AF.Square),
                reads=(rp,), writes=(rsb,))
            S.step("vector", lambda eng, sq=sq: eng.reduce_sum(
                out=rms3[:, 0:1], in_=sq[:, 0:256], axis=AX.X),
                reads=(rsq,), writes=("rms3",))
            S.step("vector", lambda eng, sb_=sb_: eng.reduce_sum(
                out=rms3[:, 1:2], in_=sb_[:, 0:16], axis=AX.X),
                reads=(rsb, "rms3"), writes=("rms3",))
            S.step("vector", lambda eng, sb_=sb_: eng.reduce_sum(
                out=rms3[:, 2:3], in_=sb_[:, 16:32], axis=AX.X),
                reads=(rsb, "rms3"), writes=("rms3",))
            S.step("vector", lambda eng: eng.tensor_tensor(
                out=rms3[:, 0:3], in0=rms3[:, 0:3], in1=t["cconst"][:, 4:7],
                op=ALU.mult), reads=("rms3", "cconst"), writes=("rms3",))
            S.step("scalar", lambda eng: eng.activation(
                out=rms3[:, 0:3], in_=rms3[:, 0:3], func=AF.Ln,
                bias=t["cconst"][:, 3:4]), reads=("rms3", "cconst"), writes=("rms3",))
            S.step("scalar", lambda eng: eng.activation(
                out=rms3[:, 0:3], in_=rms3[:, 0:3], func=AF.Exp,
                scale=t["cconst"][:, 2:3]), reads=("rms3", "cconst"), writes=("rms3",))
            dtn = dtnt[b]
            rdtn = f"dtnt{b}"
            S.step("vector", lambda eng, dtn=dtn, pp=pp: eng.scalar_tensor_tensor(
                out=dtn[:, 0:256], in0=pp[:, 0:256], scalar=rms3[:, 0:1],
                in1=t["dtln_sb"][:], op0=ALU.mult, op1=ALU.mult),
                reads=(rp, "rms3", "dtln_sb"), writes=(rdtn,))
            for rc in range(2):
                S.step("tensor", lambda eng, dtn=dtn, rc=rc, b=b: eng.transpose(
                    ps["t0"][:, (2 * b + rc) * 128:(2 * b + rc + 1) * 128],
                    dtn[:, rc * 128:(rc + 1) * 128], t["eye_sb"][:]),
                    reads=(rdtn, "eye_sb"), writes=("t0",), tag="dtnT")
                S.step("vector", lambda eng, rc=rc, b=b, tc_g=tc_g: eng.tensor_copy(
                    out=dtn_fm[:, rc * LPRE + tc_g * 128: rc * LPRE + (tc_g + 1) * 128],
                    in_=ps["t0"][:, (2 * b + rc) * 128:(2 * b + rc + 1) * 128]),
                    reads=("t0",), writes=(f"dtn_fm/{rc}c{tc_g}",))
            bn, cn = bnt[2 * b], bnt[2 * b + 1]
            S.step("vector", lambda eng, bn=bn, pp=pp: eng.scalar_tensor_tensor(
                out=bn[:, 0:16], in0=pp[:, 256:272],
                scalar=rms3[:, 1:2], in1=t["bln_sb"][:], op0=ALU.mult, op1=ALU.mult),
                reads=(rp, "rms3", "bln_sb"), writes=(f"bnt{2 * b}",))
            S.step("vector", lambda eng, cn=cn, pp=pp: eng.scalar_tensor_tensor(
                out=cn[:, 0:16], in0=pp[:, 272:288],
                scalar=rms3[:, 2:3], in1=t["cln_sb"][:], op0=ALU.mult, op1=ALU.mult),
                reads=(rp, "rms3", "cln_sb"), writes=(f"bnt{2 * b + 1}",))
            S.dma("gpsimd", ap["bn_st"][tc_g * 128:(tc_g + 1) * 128, :], bn[:, 0:16],
                  reads=(f"bnt{2 * b}",), writes=(f"bn_st/{tc_g}",), stream=f"qg_bn{b}")
            S.dma("gpsimd", ap["cn_st"][tc_g * 128:(tc_g + 1) * 128, :], cn[:, 0:16],
                  reads=(f"bnt{2 * b + 1}",), writes=(f"cn_st/{tc_g}",), stream=f"qg_cn{b}")
    S.hard_barrier()


def emit_delta(S, t, A, ap, ps):
    wdtb = [A.view(i * 16384, BF16, 8192) for i in range(2)]
    mask = A.view(32768, F32, LPRE)
    zt = [A.view(40960 + i * 2048, F32, TB) for i in range(2)]
    at = [A.view(45056 + i * 2048, F32, TB) for i in range(2)]
    dt_ = [A.view(49152 + i * 2048, F32, TB) for i in range(2)]
    dtn_fm = A.view(OFF_DTNFM, BF16, 2 * LPRE)
    for rc in range(2):
        S.dma("gpsimd", wdtb[rc][:, 0:8192], ap["wdt_t"][rc * 128:(rc + 1) * 128, :],
              writes=(f"wdtb{rc}",), stream=f"qg_wdt{rc}")
    S.dma("gpsimd", mask[:, 0:LPRE], ap["maskrep"], writes=("mask",), stream="qg_mask")
    pcnt = 0
    for dc in range(DC):
        for tb in range(4):
            par = pcnt % 2
            pcnt += 1
            pp, rp = (ps["p0"], "p0") if par == 0 else (ps["p1"], "p1")
            S.step("tensor", lambda eng, dc=dc, tb=tb, pp=pp: [
                eng.matmul(out=pp[:, 0:TB],
                           lhsT=wdtb[rc][:, dc * 128:(dc + 1) * 128],
                           rhs=dtn_fm[:, rc * LPRE + tb * TB: rc * LPRE + (tb + 1) * TB],
                           start=(rc == 0), stop=(rc == 1))
                for rc in range(2)][-1],
                reads=("wdtb0", "wdtb1") + tuple(
                    f"dtn_fm/{rc}c{tc}" for rc in range(2)
                    for tc in range(tb * 4, (tb + 1) * 4)),
                writes=(rp,), tag="mmdl")
            z, a, d_ = zt[par], at[par], dt_[par]
            rz, ra, rdl = f"zt{par}", f"at{par}", f"dt{par}"
            S.step("vector", lambda eng, pp=pp, z=z, dc=dc: eng.tensor_scalar(
                out=z, in0=pp[:, 0:TB], scalar1=t["dtb_sb"][:, dc:dc + 1],
                scalar2=None, op0=ALU.add), reads=(rp, "dtb_sb"), writes=(rz,))
            S.step("scalar", lambda eng, z=z, a=a: eng.activation(
                out=a, in_=z, func=AF.Abs), reads=(rz,), writes=(ra,))
            S.step("scalar", lambda eng, a=a: eng.activation(
                out=a, in_=a, func=AF.Exp, scale=t["cconst"][:, 7:8]),
                reads=(ra, "cconst"), writes=(ra,))
            S.step("scalar", lambda eng, a=a: eng.activation(
                out=a, in_=a, func=AF.Ln, bias=t["cconst"][:, 1:2]),
                reads=(ra, "cconst"), writes=(ra,))
            S.step("vector", lambda eng, z=z: eng.tensor_scalar(
                out=z, in0=z, scalar1=0.0, scalar2=None, op0=ALU.max),
                reads=(rz,), writes=(rz,))
            S.step("vector", lambda eng, z=z, a=a, d_=d_: eng.tensor_tensor(
                out=d_, in0=z, in1=a, op=ALU.add), reads=(rz, ra), writes=(rdl,))
            S.step("vector", lambda eng, d_=d_, tb=tb: eng.tensor_tensor(
                out=d_, in0=d_, in1=mask[:, tb * TB:(tb + 1) * TB],
                op=ALU.mult), reads=(rdl, "mask"), writes=(rdl,))
            S.dma("gpsimd", ap["dl_s"][dc * 128:(dc + 1) * 128, tb * TB:(tb + 1) * TB],
                  d_, reads=(rdl,), writes=(f"dl_s/{dc}t{tb}",), stream=f"qg_dl{par}")
    S.hard_barrier()


def emit_scan(S, t, A, ap, ps):
    dA = A.view(0, F32, 8192)
    X = A.view(32768, F32, 8192)
    St = A.view(65536, F32, 8192)
    Bx = A.view(98304, F32, 8192)
    Cx = A.view(131072, F32, 8192)
    dlt = [A.view(163840 + i * 2048, F32, TB) for i in range(2)]
    u2l = [A.view(167936 + i * 2048, F32, TB) for i in range(2)]
    sgl = [A.view(172032 + i * 2048, F32, TB) for i in range(2)]
    yt = A.view(176128, F32, TB)
    ymo = [A.view(178176 + i * 1024, BF16, TB) for i in range(2)]
    bnrow = t["lnrows"]
    S.step("vector", lambda eng: eng.memset(t["carry"][:], 0.0),
           reads=(), writes=("carry",))
    for lseg in range(4):
        own = lseg >= 2
        l0 = lseg * LSEG
        pairs = [("bn_st", Bx, "Bx")] + ([("cn_st", Cx, "Cx")] if own else [])
        for tensor_name, dstv, dres in pairs:
            src = ap[tensor_name]
            flat = src.rearrange("l n -> (l n)").unsqueeze(0)
            for j in range(16):
                c0 = l0 * NST + j * 512
                S.dma("gpsimd", bnrow[0:1, 0:512], flat[:, c0:c0 + 512],
                      reads=("bn_all",), writes=("bnrow",), stream="qg_bnr")
                pb = ps["p5"]
                S.step("tensor", lambda eng: eng.matmul(
                    out=pb[:, 0:TB], lhsT=t["ones_row"][:], rhs=bnrow[0:1, 0:512],
                    start=True, stop=True), reads=("bnrow", "ones_row"),
                    writes=("p5",), tag="bcastB")
                S.step("vector", lambda eng, dstv=dstv, j=j: eng.tensor_copy(
                    out=dstv[:, j * 512:(j + 1) * 512], in_=pb[:, 0:TB]),
                    reads=("p5",), writes=(dres,))
        for dc in range(DC):
            b = dc % 2
            dl, u2, sg = dlt[b], u2l[b], sgl[b]
            rdl, ru2, rsg = f"dlt{b}", f"u2l{b}", f"sgl{b}"
            rows = slice(dc * 128, (dc + 1) * 128)
            S.dma("gpsimd", dl, ap["dl_s"][rows, l0:l0 + LSEG],
                  reads=(f"dl_s/{dc}t{lseg}",), writes=(rdl,), stream=f"qg_sd{b}")
            S.dma("gpsimd", u2, ap["u2_s"][rows, l0:l0 + LSEG],
                  reads=(f"u2_s/{dc}h{lseg // 2}",), writes=(ru2,), stream=f"qg_su{b}")
            if own:
                S.dma("gpsimd", sg, ap["sg_s"][rows, l0 - 1024:l0 - 1024 + LSEG],
                      reads=(f"sg_s/{dc}t{lseg - 2}",), writes=(rsg,), stream=f"qg_sg{b}")
            for n in range(NST):
                S.step("scalar", lambda eng, dl=dl, n=n, dc=dc: eng.activation(
                    out=dA[:, 0:8192].rearrange("p (l n) -> p l n", n=NST)[:, :, n],
                    in_=dl, func=AF.Exp,
                    scale=t["a_sb"][:, dc * NST + n:dc * NST + n + 1]),
                    reads=(rdl, "a_sb"), writes=("dA",), tag="dAexp")
            S.step("vector", lambda eng, dl=dl, u2=u2: eng.tensor_tensor(
                out=dl, in0=dl, in1=u2, op=ALU.mult),
                reads=(rdl, ru2), writes=(rdl,))
            S.step("vector", lambda eng, dl=dl: eng.tensor_tensor(
                out=X[:, 0:8192].rearrange("p (l n) -> p l n", n=NST),
                in0=dl.unsqueeze(2).broadcast_to([128, LSEG, NST]),
                in1=Bx[:, 0:8192].rearrange("p (l n) -> p l n", n=NST),
                op=ALU.mult), reads=(rdl, "Bx"), writes=("X",))
            for n in range(NST):
                S.step("vector", lambda eng, n=n, dc=dc: eng.tensor_tensor_scan(
                    out=St[:, 0:8192].rearrange("p (l n) -> p l n", n=NST)[:, :, n],
                    data0=dA[:, 0:8192].rearrange("p (l n) -> p l n", n=NST)[:, :, n],
                    data1=X[:, 0:8192].rearrange("p (l n) -> p l n", n=NST)[:, :, n],
                    initial=t["carry"][:, dc * NST + n:dc * NST + n + 1],
                    op0=ALU.mult, op1=ALU.add),
                    reads=("dA", "X", "carry"), writes=("St",), tag="scan")
            S.step("vector", lambda eng, dc=dc: eng.tensor_copy(
                out=t["carry"][:, dc * NST:(dc + 1) * NST], in_=St[:, 8176:8192]),
                reads=("St",), writes=("carry",))
            if own:
                S.step("vector", lambda eng: eng.tensor_tensor(
                    out=X[:, 0:8192], in0=St[:, 0:8192], in1=Cx[:, 0:8192],
                    op=ALU.mult), reads=("St", "Cx"), writes=("X",))
                S.step("vector", lambda eng: eng.reduce_sum(
                    out=yt[:, 0:TB],
                    in_=X[:, 0:8192].rearrange("p (l n) -> p l n", n=NST),
                    axis=AX.X), reads=("X",), writes=("yt",))
                S.step("vector", lambda eng, u2=u2, dc=dc: eng.scalar_tensor_tensor(
                    out=yt[:, 0:TB], in0=u2, scalar=t["dpar_sb"][:, dc:dc + 1],
                    in1=yt[:, 0:TB], op0=ALU.mult, op1=ALU.add),
                    reads=(ru2, "yt", "dpar_sb"), writes=("yt",))
                ym = ymo[b]
                S.step("vector", lambda eng, sg=sg, ym=ym: eng.tensor_tensor(
                    out=ym, in0=yt[:, 0:TB], in1=sg, op=ALU.mult),
                    reads=("yt", rsg), writes=(f"ymo{b}",))
                S.dma("gpsimd", ap["ym_s"][rows, (lseg - 2) * TB:(lseg - 1) * TB],
                      ym, reads=(f"ymo{b}",), writes=(f"ym_s/{dc}t{lseg - 2}",),
                      stream=f"qg_ym{b}")
    S.hard_barrier()


def emit_wout(S, t, A, ap, ps):
    ym = A.view(0, BF16, DC * 1024)
    slab = [A.view(131072 + i * 512, BF16, 256) for i in range(2)]
    x1t = [A.view(133120 + i * 2048, F32, TB) for i in range(2)]
    x2t = [A.view(137216 + i * 2048, F32, TB) for i in range(2)]
    for dc in range(DC):
        S.dma("sync", ym[:, dc * 1024:(dc + 1) * 1024],
              ap["ym_s"][dc * 128:(dc + 1) * 128, :],
              reads=(f"ym_s/{dc}t0", f"ym_s/{dc}t1"), writes=(f"ymr{dc}",),
              stream="qs_ym")
    S.fence("tensor", ["qs_ym"], [f"ymr{dc}" for dc in range(DC)])
    pd = [ps["p0"], ps["p1"], ps["p2"], ps["p3"]]
    for hcg in range(16):
        for dc in range(DC):
            sl = slab[dc % 2]
            rsl = f"slabO{dc % 2}"
            S.dma("gpsimd", sl, ap["wout_t"][dc * 128:(dc + 1) * 128,
                                             hcg * 256:(hcg + 1) * 256],
                  writes=(rsl,), stream=f"qg_wo{dc % 2}")
            S.step("tensor", lambda eng, sl=sl, dc=dc: [
                eng.matmul(out=pd[2 * sub + tb][:, 0:TB],
                           lhsT=sl[:, sub * 128:(sub + 1) * 128],
                           rhs=ym[:, dc * 1024 + tb * TB: dc * 1024 + (tb + 1) * TB],
                           start=(dc == 0), stop=(dc == DC - 1))
                for sub in range(2) for tb in range(2)][-1],
                reads=(rsl, f"ymr{dc}"), writes=("p0", "p1", "p2", "p3"), tag="mmwo")
        for sub in range(2):
            hc = hcg * 2 + sub
            for tb in range(2):
                par = tb % 2
                xb, ob = x1t[par], x2t[par]
                S.dma("gpsimd", xb,
                      ap["x1_s"][hc * 128:(hc + 1) * 128,
                                 1024 + tb * TB:1024 + (tb + 1) * TB],
                      reads=(f"x1_s/{hc}",), writes=(f"x1t{par}",), stream=f"qg_x1{par}")
                S.step("vector", lambda eng, xb=xb, ob=ob, sub=sub, tb=tb: eng.tensor_tensor(
                    out=ob, in0=pd[2 * sub + tb][:, 0:TB], in1=xb, op=ALU.add),
                    reads=(f"p{2 * sub + tb}", f"x1t{par}"), writes=(f"x2t{par}",))
                S.dma("gpsimd", ap["x2_s"][hc * 128:(hc + 1) * 128, tb * TB:(tb + 1) * TB],
                      ob, reads=(f"x2t{par}",), writes=(f"x2_s/{hc}",),
                      stream=f"qg_x2{par}")
    S.hard_barrier()


INPUT_SPECS = {
    "x_t": ([HID, LPRE], F32),
    "win_t": ([4096, 16384], F32),
    "wout_t": ([8192, 4096], F32),
    "wx_t": ([8192, 288], F32),
    "wdt_t": ([256, 8192], F32),
    "a_pack": ([128, DC * NST], F32),
    "conv_pack": ([128, DC * 4], F32),
    "convb_pack": ([128, DC], F32),
    "dtb_pack": ([128, DC], F32),
    "dpar_pack": ([128, DC], F32),
    "dtln_rep": ([128, 256], F32),
    "bln_rep": ([128, 16], F32),
    "cln_rep": ([128, 16], F32),
    "maskrep": ([128, LPRE], F32),
    "qw_g": ([11008, 512], I32), "sc_g": ([11008, 32], F32),
    "qz_g": ([32, 1376], I32), "sg_g": ([32, 11008], F32),
    "qw_u": ([11008, 512], I32), "sc_u": ([11008, 32], F32),
    "qz_u": ([32, 1376], I32), "sg_u": ([32, 11008], F32),
    "qw_d": ([4096, 1376], I32), "sc_d": ([4096, 86], F32),
    "qz_d": ([86, 512], I32), "sg_d": ([86, 4096], F32),
    "ones_col": ([128, 1], F32),
    "ones_row": ([1, 128], F32),
    "eye_bf": ([128, 128], BF16),
    "eye_f32": ([128, 128], F32),
    "cconst": ([128, 8], F32),
}

SCRATCH_SPECS = {
    "wg_s": ([11008, 4096], BF16),
    "wu_s": ([11008, 4096], BF16),
    "wd_s": ([4096, 11008], BF16),
    "h1_s": ([INTER, LPRE], BF16),
    "x1_s": ([HID, LPRE], F32),
    "x2_s": ([HID, LOWN], F32),
    "ur_s": ([DIN, LPRE], F32),
    "u2_s": ([DIN, LPRE], F32),
    "sg_s": ([DIN, LOWN], F32),
    "dl_s": ([DIN, LPRE], F32),
    "ym_s": ([DIN, LOWN], BF16),
    "bn_st": ([LPRE, NST], F32),
    "cn_st": ([LPRE, NST], F32),
    "szt_g": ([11008, 32], F32),
    "szt_u": ([11008, 32], F32),
    "szt_d": ([4096, 86], F32),
}

_SB_MAP = (("ones_col", "ones_col"), ("ones_row", "ones_row"),
           ("cconst", "cconst"), ("eye_bf", "eye_sb"), ("eye_f32", "eye_f32"), ("dtln_rep", "dtln_sb"),
           ("bln_rep", "bln_sb"), ("cln_rep", "cln_sb"), ("conv_pack", "conv_sb"),
           ("convb_pack", "convb_sb"), ("dtb_pack", "dtb_sb"),
           ("dpar_pack", "dpar_sb"), ("a_pack", "a_sb"))


def build_program(phases=("all",), debug_outputs=()):
    nc = bass.Bass("TRN2", target_bir_lowering=False)
    ap = {}
    for name, (shape, dt) in INPUT_SPECS.items():
        ap[name] = nc.dram_tensor(name, shape, dt, kind="ExternalInput").ap()
    ap["out_t"] = nc.dram_tensor("out_t", [HID, LOWN], F32, kind="ExternalOutput").ap()
    for name, (shape, dt) in SCRATCH_SPECS.items():
        kind = "ExternalOutput" if name in debug_outputs else "Internal"
        ap[name] = nc.dram_tensor(name, shape, dt, kind=kind).ap()

    S = Sched(nc)
    A = Arena(nc)
    t = {}
    for nm, sb_nm in _SB_MAP:
        shp, dt = INPUT_SPECS[nm]
        t[sb_nm] = nc.alloc_sbuf_tensor(sb_nm + "_sb", shp, dt)
        S.dma("sync", t[sb_nm][:], ap[nm], writes=(sb_nm,), stream="qs_c")
    S.fence("vector", ["qs_c"], [s for _, s in _SB_MAP])
    t["lnrows"] = nc.alloc_sbuf_tensor("lnrows", [1, 2 * TB], F32)[:]
    t["carry"] = nc.alloc_sbuf_tensor("carry", [128, DC * NST], F32)
    t["lntmp"] = [A.view(OFF_LNT + i * 2048, F32, TB) for i in range(4)]
    t["sil"] = [A.view(OFF_SIL + i * 2048, F32, TB) for i in range(2)]
    t["hout"] = [A.view(OFF_HOUT + i * 1024, BF16, TB) for i in range(4)]
    ps = {f"p{i}": nc.alloc_psum_tensor(f"ps{i}", [128, TB], F32) for i in range(6)}
    ps["t0"] = nc.alloc_psum_tensor("pst0", [128, 1024], BF16)

    def want(p):
        return "all" in phases or p in phases

    if want("dequant"):
        emit_dequant(S, t, A, ap, ps, "g", 11008, 4096, 32)
        emit_dequant(S, t, A, ap, ps, "u", 11008, 4096, 32)
        emit_dequant(S, t, A, ap, ps, "d", 4096, 11008, 86)
        S.hard_barrier()
    if want("ffn1"):
        emit_ffn(S, t, A, ap, ps, ap["x_t"], ap["x1_s"],
                 lambda kc: f"x_t/{kc}", lambda hc, tb: f"x1_s/{hc}",
                 (0, 1, 2, 3), "F1")
    if want("win"):
        emit_win(S, t, A, ap, ps)
    if want("conv"):
        emit_conv_ssm(S, t, A, ap, ps)
    if want("delta"):
        emit_delta(S, t, A, ap, ps)
    if want("scan"):
        emit_scan(S, t, A, ap, ps)
    if want("wout"):
        emit_wout(S, t, A, ap, ps)
    if want("ffn2"):
        emit_ffn(S, t, A, ap, ps, ap["x2_s"], ap["out_t"],
                 lambda kc: f"x2_s/{kc}", lambda hc, tb: f"out_t/{hc}",
                 (0, 1), "F2")

    S.hard_barrier()
    for stream, val in S.dcnt.items():
        sem = S.dsem[stream]
        S.raw("sync", lambda eng, s=sem, v=val: eng.wait_ge(s, v))
    S.finalize()
    return nc


def _pack_rows(a, inner=1):
    return np.ascontiguousarray(
        a.reshape(DC, 128, inner).transpose(1, 0, 2).reshape(128, DC * inner))


def _prep_shared(i):
    sh = {}
    sh["win_t"] = np.ascontiguousarray(i["L0_attn_W_in"].T)
    sh["wout_t"] = np.ascontiguousarray(i["L0_attn_W_out"].T)
    sh["wx_t"] = np.ascontiguousarray(i["L0_attn_Wx"].T)
    sh["wdt_t"] = np.ascontiguousarray(i["L0_attn_Wdt"].T)
    sh["a_pack"] = _pack_rows(np.asarray(i["L0_attn_A"]), NST)
    sh["conv_pack"] = _pack_rows(np.asarray(i["L0_attn_conv_w"]), 4)
    sh["convb_pack"] = _pack_rows(np.asarray(i["L0_attn_conv_b"]).reshape(-1, 1), 1)
    sh["dtb_pack"] = _pack_rows(np.asarray(i["L0_attn_dt_bias"]).reshape(-1, 1), 1)
    sh["dpar_pack"] = _pack_rows(np.asarray(i["L0_attn_D_param"]).reshape(-1, 1), 1)
    sh["dtln_rep"] = np.ascontiguousarray(
        np.broadcast_to(np.asarray(i["L0_attn_dt_ln_w"]).reshape(1, 256), (128, 256)))
    sh["bln_rep"] = np.ascontiguousarray(
        np.broadcast_to(np.asarray(i["L0_attn_b_ln_w"]).reshape(1, 16), (128, 16)))
    sh["cln_rep"] = np.ascontiguousarray(
        np.broadcast_to(np.asarray(i["L0_attn_c_ln_w"]).reshape(1, 16), (128, 16)))
    for mat, pre in (("g", "gate"), ("u", "up"), ("d", "down")):
        qw = np.asarray(i[f"L0_ffn_{pre}_qweight"])
        scl = np.asarray(i[f"L0_ffn_{pre}_scales"])
        qz = np.asarray(i[f"L0_ffn_{pre}_qzeros"])
        sh[f"qw_{mat}"] = np.ascontiguousarray(qw.T)
        sh[f"sc_{mat}"] = np.ascontiguousarray(scl.T)
        sh[f"qz_{mat}"] = np.ascontiguousarray(qz)
        sh[f"sg_{mat}"] = np.ascontiguousarray(scl)
    sh["ones_col"] = np.ones((128, 1), np.float32)
    sh["ones_row"] = np.ones((1, 128), np.float32)
    sh["eye_bf"] = np.eye(128).astype(bfdt)
    sh["eye_f32"] = np.eye(128, dtype=np.float32)
    cc = np.zeros((128, 8), np.float32)
    cc[:, 0] = EPS_LN
    cc[:, 1] = 1.0
    cc[:, 2] = -0.5
    cc[:, 3] = EPS_RMS
    cc[:, 4] = 1.0 / RDT
    cc[:, 5] = 1.0 / NST
    cc[:, 6] = 1.0 / NST
    cc[:, 7] = -1.0
    sh["cconst"] = cc
    return sh


def _core_in_map(i, sh, b, h):
    m = dict(sh)
    x = np.asarray(i["L0_attn_hidden_states"][b], dtype=np.float32)
    if h == 0:
        xc = np.zeros((LPRE, HID), np.float32)
        xc[1024:] = x[0:1024]
        mask = np.zeros((1, LPRE), np.float32)
        mask[:, 1024:] = 1.0
    else:
        xc = x
        mask = np.ones((1, LPRE), np.float32)
    m["x_t"] = np.ascontiguousarray(xc.T)
    m["maskrep"] = np.ascontiguousarray(np.broadcast_to(mask, (128, LPRE)))
    return m


def kernel(**inputs):
    sh = _prep_shared(inputs)
    nc = build_program()
    in_maps = [_core_in_map(inputs, sh, c // 2, c % 2) for c in range(8)]
    res = run_bass_kernel_spmd(nc, in_maps, core_ids=list(range(8)))
    out = np.empty((4, 2048, 4096), np.float32)
    for c in range(8):
        b, h = c // 2, c % 2
        out[b, h * LOWN:(h + 1) * LOWN, :] = np.asarray(res.results[c]["out_t"]).T
    return out
